# version 06
# baseline (speedup 1.0000x reference)
import os
import numpy as np
import ml_dtypes
import concourse.bass as bass
import concourse.mybir as mybir
from concourse.bass_utils import run_bass_kernel_spmd

F32 = mybir.dt.float32
BF16 = mybir.dt.bfloat16
I32 = mybir.dt.int32
AF = mybir.ActivationFunctionType
ALU = mybir.AluOpType
AX = mybir.AxisListType

P = 128
T = 1024
NF = 16
D = 2048
FW_NORM = float(np.float32(1.0 / np.sqrt(8192.0)))
EPS = 1e-5
SMC = 20.0
STAGE = os.environ.get("KSTAGE", "full")
_SORDER = ("l0a", "l0f", "s3", "s4", "s5", "s6", "l1a", "full")


def stage_ge(s):
    idx = _SORDER.index(STAGE) if STAGE in _SORDER else len(_SORDER)
    return idx >= _SORDER.index(s)

ENGINES = ("sync", "gpsimd", "vector", "scalar", "tensor")


class Em:

    DSEMS = ("d_ld", "d_gld", "d_wb0", "d_wb1", "d_kb0", "d_kb1", "d_misc", "d_out", "d_xbf", "d_fb0", "d_fb1", "d_fb2")
    DSEM_CHAINED = ("d_misc",)

    def __init__(self):
        self.ops = {e: [] for e in ENGINES}
        self.cnt = {e: 0 for e in ENGINES}
        for s in self.DSEMS:
            self.cnt[s] = 0

    AUTO_CHAIN = ("vector", "scalar", "gpsimd")

    def emit(self, eng, fn, waits=(), sig=None):
        if sig is None:
            sig = 1 if eng in self.AUTO_CHAIN else 0
        waits = [w for w in waits if w is not None]
        if eng in self.AUTO_CHAIN and self.cnt[eng] > 0:
            waits.append((eng, self.cnt[eng]))
            sig = 1
        self.ops[eng].append([fn, tuple(waits), sig, eng])
        if sig:
            self.cnt[eng] += sig
        return (eng, self.cnt[eng])

    def dma(self, eng, sem, out, in_, waits=()):
        waits = [w for w in waits if w is not None]
        if sem in self.DSEM_CHAINED and self.cnt[sem] > 0:
            waits.append((sem, self.cnt[sem]))
        self.ops[eng].append([
            lambda e, o=out, i=in_: e.dma_start(out=o, in_=i),
            tuple(waits), 16, sem])
        self.cnt[sem] += 16
        return (sem, self.cnt[sem])

    def tok(self, eng):
        return (eng, self.cnt[eng])

    def build(self, sems, block):
        def mk(engname, dec):
            def _sec(eng):
                hw = {}
                for fn, waits, sig, sem in self.ops[engname]:
                    for (pe, val) in waits:
                        if hw.get(pe, -1) >= val:
                            continue
                        hw[pe] = val
                        eng.wait_ge(sems[pe], val)
                    inst = fn(eng)
                    if sig:
                        inst.then_inc(sems[sem], sig)
            dec(_sec)
        mk("sync", block.sync)
        mk("gpsimd", block.gpsimd)
        mk("vector", block.vector)
        mk("scalar", block.scalar)
        mk("tensor", block.tensor)


def build_program():
    nc = bass.Bass("TRN2", target_bir_lowering=False)
    din = {}

    def dinp(name, shape, dt=F32):
        din[name] = nc.dram_tensor(name, list(shape), dt, kind="ExternalInput").ap()
        return din[name]

    dinp("xT", [D, T + 1])
    for w in ("wv0", "wk0", "wr0", "wo0", "wq1", "wk1", "wv1", "wo1"):
        dinp(w, [NF, D, P])
    dinp("td", [P, NF]); dinp("tf", [P, NF])
    dinp("tmk", [P, NF]); dinp("tmv", [P, NF]); dinp("tmr", [P, NF])
    dinp("bqkvo", [P, 4 * NF])
    dinp("ropes", [P, 4, T])
    dinp("tri", [P, P])
    dinp("hadp", [P, P]); dinp("hadn", [P, P])
    dinp("perm", [P, P])
    dinp("pvflag", [P, 2])
    for wn in ("wg0_sl", "wu0_sl", "wd0_sl", "wg1_sl", "wu1_sl", "wd1_sl"):
        dinp(wn, [P, 16384])

    y = nc.dram_tensor("y", [NF, P, T], F32, kind="ExternalOutput").ap()

    ar_in = nc.dram_tensor("ar_in", [1, 8], F32, kind="Internal").ap()
    ar_out = nc.dram_tensor("ar_out", [1, 8], F32, kind="Internal").ap()
    ag = {}
    for wn in ("g0", "u0", "d0", "g1", "u1", "d1"):
        ag[wn + "i"] = nc.dram_tensor("t%si" % wn, [P, 16384], BF16, kind="Internal").ap()
        ag[wn + "o"] = nc.dram_tensor("t%so" % wn, [8 * P, 16384], BF16, kind="Internal",
                                      addr_space="Shared").ap()
    sb_in = nc.dram_tensor("sb_in", [P, NF], F32, kind="Internal").ap()
    sb_out = nc.dram_tensor("sb_out", [2 * P, NF], F32, kind="Internal").ap()
    agk_in = nc.dram_tensor("agk_in", [D, T], BF16, kind="Internal").ap()
    agk_out = nc.dram_tensor("agk_out", [2 * D, T], BF16, kind="Internal").ap()
    agv_in = nc.dram_tensor("agv_in", [P, 32 * 8 * 65], BF16, kind="Internal").ap()
    agv_out = nc.dram_tensor("agv_out", [2 * P, 32 * 8 * 65], BF16,
                             kind="Internal").ap()

    em = Em()

    from contextlib import ExitStack
    with ExitStack() as _st:
        Xh = _st.enter_context(nc.sbuf_tensor("X", [P, NF, T + 1], F32))
        Ah = _st.enter_context(nc.sbuf_tensor("A", [P, 16640], BF16))
        BQh = _st.enter_context(nc.sbuf_tensor("BQ", [P, 4 * NF], F32))
        Bh = _st.enter_context(nc.sbuf_tensor("B", [P, NF * (T + 1)], BF16))
        CDh = _st.enter_context(nc.sbuf_tensor("CD", [P, 32 * T], BF16))
        WBh = _st.enter_context(nc.sbuf_tensor("WB", [P, 2, NF, P], BF16))
        VECh = _st.enter_context(nc.sbuf_tensor("VEC", [P, 160], F32))
        I32Sh = _st.enter_context(nc.sbuf_tensor("I32S", [P, 512], I32))
        IDBh = _st.enter_context(nc.sbuf_tensor("IDB", [P, P], BF16))
        IDFh = _st.enter_context(nc.sbuf_tensor("IDF", [P, P], F32))
        TRIh = _st.enter_context(nc.sbuf_tensor("TRI", [P, P], BF16))
        HADh = _st.enter_context(nc.sbuf_tensor("HAD", [P, 2, P], BF16))
        PERMh = _st.enter_context(nc.sbuf_tensor("PERM", [P, P], BF16))
        ONESCh = _st.enter_context(nc.sbuf_tensor("ONESC", [P, 1], BF16))
        ONESCFh = _st.enter_context(nc.sbuf_tensor("ONESCF", [P, 1], F32))
        ONE1Fh = _st.enter_context(nc.sbuf_tensor("ONE1F", [1, P], F32))
        ONE1Bh = _st.enter_context(nc.sbuf_tensor("ONE1B", [1, P], BF16))
        PS01h = _st.enter_context(nc.psum_tensor("PS01", [P, 1024], F32))
        PS23h = _st.enter_context(nc.psum_tensor("PS23", [P, 1024], F32))
        PS4h = _st.enter_context(nc.psum_tensor("PS4", [P, 512], F32))
        PS5h = _st.enter_context(nc.psum_tensor("PS5", [P, 512], F32))
        PS6h = _st.enter_context(nc.psum_tensor("PS6", [P, 512], F32))
        PSBh = _st.enter_context(nc.psum_tensor("PSB", [P, 1024], BF16))
        s_sync = _st.enter_context(nc.semaphore("s_sync"))
        s_gpsimd = _st.enter_context(nc.semaphore("s_gpsimd"))
        s_vector = _st.enter_context(nc.semaphore("s_vector"))
        s_scalar = _st.enter_context(nc.semaphore("s_scalar"))
        s_tensor = _st.enter_context(nc.semaphore("s_tensor"))
        dsems = {}
        for _dn in Em.DSEMS:
            dsems[_dn] = _st.enter_context(nc.semaphore(_dn))
        block = _st.enter_context(nc.Block())
        sems = {"sync": s_sync, "gpsimd": s_gpsimd, "vector": s_vector,
                "scalar": s_scalar, "tensor": s_tensor}
        sems.update(dsems)
        X = Xh.ap(); A2full = Ah.ap(); A2 = A2full[:, 0:NF * (T + 1)]
        B2 = Bh.ap(); CD = CDh.ap(); WB = WBh.ap(); BQ = BQh.ap()
        A = A2.rearrange("p (s t) -> p s t", t=T + 1)
        B = B2.rearrange("p (s t) -> p s t", t=T + 1)
        VEC = VECh.ap(); I32S = I32Sh.ap(); IDB = IDBh.ap(); IDF = IDFh.ap()
        TRI = TRIh.ap(); HAD = HADh.ap(); PERM = PERMh.ap(); ONESC = ONESCh.ap()
        ONESCF = ONESCFh.ap(); ONE1F = ONE1Fh.ap(); ONE1B = ONE1Bh.ap()
        PS01 = PS01h.ap(); PS23 = PS23h.ap(); PS4 = PS4h.ap(); PS5 = PS5h.ap()
        PS6 = PS6h.ap(); PSB = PSBh.ap()

        Af = A2.bitcast(F32).rearrange("p (s t) -> p s t", t=T + 1)
        Bff = B2.bitcast(F32)
        CDf = CD.bitcast(F32)
        Ct = CD.rearrange("p (s t) -> p s t", t=T)
        Zt = CD.rearrange("p (s t) -> p s t", t=512)
        CDflat = CD
        XT = X[:, :, 1:T + 1]

        e = em

        def VS(a, b):
            return VEC[:, a:b]

        tok_x = e.dma("sync", "d_ld", X[:, :, :], din["xT"].rearrange("(i p) t -> p i t", p=P))
        e.dma("sync", "d_ld", VS(0, 16), din["td"])
        e.dma("sync", "d_ld", VS(16, 32), din["tf"])
        e.dma("sync", "d_ld", VS(32, 48), din["tmk"])
        e.dma("sync", "d_ld", VS(48, 64), din["tmv"])
        e.dma("sync", "d_ld", VS(64, 80), din["tmr"])
        tok_vec = e.dma("sync", "d_ld", VS(112, 114), din["pvflag"])
        tok_bq = e.dma("sync", "d_ld", BQ[:, :], din["bqkvo"])

        g1 = e.emit("gpsimd", lambda g: g.iota(IDF[:, :], [[1, P]], base=0,
                    channel_multiplier=0, allow_small_or_imprecise_dtypes=True), sig=1)
        g2 = e.emit("gpsimd", lambda g: g.iota(VS(146, 147), [[0, 1]], base=0,
                    channel_multiplier=1, allow_small_or_imprecise_dtypes=True), sig=1)
        vid = e.emit("vector", lambda v: v.tensor_scalar(
            IDB[:, :], IDF[:, :], VS(146, 147), None, ALU.is_equal),
            waits=[g1, g2], sig=1)
        e.emit("vector", lambda v: v.tensor_scalar(
            IDF[:, :], IDF[:, :], VS(146, 147), None, ALU.is_equal),
            waits=[vid], sig=1)
        e.emit("vector", lambda v: v.memset(ONESC[:, :], 1.0), sig=1)
        e.emit("vector", lambda v: v.memset(ONESCF[:, :], 1.0), sig=1)
        e.emit("vector", lambda v: v.memset(ONE1F[:, :], 1.0), sig=1)
        tok_ones = e.emit("vector", lambda v: v.memset(ONE1B[:, :], 1.0), sig=1)
        e.dma("gpsimd", "d_gld", TRI[:, :], din["tri"])
        e.dma("gpsimd", "d_gld", HAD[:, 0, :], din["hadp"])
        e.dma("gpsimd", "d_gld", HAD[:, 1, :], din["hadn"])
        tok_perm = e.dma("gpsimd", "d_gld", PERM[:, :], din["perm"])
        tok_ld = e.tok("d_ld")
        tok_gld = e.tok("d_gld")
        s1 = e.emit("scalar", lambda s: s.activation(VS(96, 112), VS(0, 16), AF.Exp),
                    waits=[tok_ld], sig=1)
        v1 = e.emit("vector", lambda v: v.tensor_scalar(VS(80, 96), VS(96, 112), -1.0,
                    None, ALU.mult), waits=[s1], sig=1)
        tok_w = e.emit("scalar", lambda s: s.activation(VS(96, 112), VS(80, 96), AF.Exp),
                       waits=[v1], sig=1)

        wnames = ["g0", "u0", "d0", "g1", "u1", "d1"]
        slnames = ["wg0_sl", "wu0_sl", "wd0_sl", "wg1_sl", "wu1_sl", "wd1_sl"]
        do_ffnw = stage_ge("l0f")
        tok_ffnw_a_free = None
        tok_ffnw_b_free = None
        tok_arst = None
        ag_toks = {}
        if do_ffnw:
            for wi, sl in enumerate(slnames):
                for hlf in range(2):
                    tok_l = e.dma("sync", "d_misc", Bff[:, 0:8192],
                                  din[sl][:, hlf * 8192:(hlf + 1) * 8192],
                                  waits=[e.tok("vector")])
                    dst = VS(148 + wi, 149 + wi) if hlf == 0 else VS(146, 147)
                    e.emit("vector", lambda v, d=dst: v.reduce_sum(
                        d, Bff[:, 0:8192], AX.X, apply_absolute_value=True),
                        waits=[tok_l], sig=1)
                    if hlf == 1:
                        e.emit("vector", lambda v, wi=wi: v.tensor_tensor(
                            VS(148 + wi, 149 + wi), VS(148 + wi, 149 + wi),
                            VS(146, 147), ALU.add), sig=1)
            tok_part = e.tok("vector")
            tk = e.emit("tensor", lambda t: t.matmul(
                PS4[0:6, 0:1], VS(148, 154), ONESCF[:, :], start=True, stop=True),
                waits=[tok_part, tok_ones], sig=1)
            e.emit("vector", lambda v: v.memset(VEC[0:8, 147:148], 0.0))
            tok_cp = e.emit("vector", lambda v: v.tensor_copy(
                VEC[0:6, 147:148], PS4[0:6, 0:1]), waits=[tk], sig=1)
            tok_ari = e.dma("sync", "d_misc", ar_in[0:1, 0:8], VEC[0:8, 147:148],
                            waits=[tok_cp])
            tok_ar = e.emit("gpsimd", lambda g: g.collective_compute(
                "AllReduce", ALU.add, replica_groups=[[0, 1, 2, 3, 4, 5, 6, 7]],
                ins=[ar_in], outs=[ar_out]), waits=[tok_ari], sig=1)
            thr_row = CDf[0:1, 16376:16382]
            tok_arl = e.dma("sync", "d_misc", thr_row, ar_out[0:1, 0:6], waits=[tok_ar])
            tok_thr0 = e.emit("vector", lambda v: v.tensor_scalar(
                thr_row, thr_row, 0.5 / 16777216.0, None, ALU.mult),
                waits=[tok_arl], sig=1)
            tk2 = e.emit("tensor", lambda t: t.matmul(
                PS4[:, 0:6], ONE1F[:, :], thr_row, start=True, stop=True),
                waits=[tok_thr0], sig=1)
            e.emit("vector", lambda v: v.tensor_copy(VS(148, 154), PS4[:, 0:6]),
                   waits=[tk2], sig=1)
            tok_arst = e.emit("vector", lambda v: v.tensor_scalar(
                VS(154, 160), VS(148, 154), 2.0, None, ALU.mult), sig=1)
            for wi, (wn, sl) in enumerate(zip(wnames, slnames)):
                for hlf in range(2):
                    tok_l = e.dma("sync", "d_misc", Bff[:, 0:8192],
                                  din[sl][:, hlf * 8192:(hlf + 1) * 8192],
                                  waits=[e.tok("vector")])
                    thr = VS(148 + wi, 149 + wi)
                    vg = e.emit("vector", lambda v, th=thr: v.tensor_scalar(
                        A2[:, 0:8192], Bff[:, 0:8192], th, None, ALU.is_gt),
                        waits=[tok_l, tok_arst], sig=1)
                    vg2 = e.emit("vector", lambda v, th=thr: v.scalar_tensor_tensor(
                        A2[:, 8192:16384], Bff[:, 0:8192], -1.0,
                        th.broadcast_to([P, 8192]),
                        ALU.mult, ALU.is_gt), waits=[vg], sig=1)
                    vt = e.emit("vector", lambda v, h=hlf: v.tensor_tensor(
                        CDflat[:, h * 8192:(h + 1) * 8192], A2[:, 0:8192],
                        A2[:, 8192:16384], ALU.subtract), waits=[vg2], sig=1)
                    e.dma("sync", "d_misc", ag[wn + "i"][:, hlf * 8192:(hlf + 1) * 8192],
                          CDflat[:, hlf * 8192:(hlf + 1) * 8192], waits=[vt])
                tok_w_done = e.tok("d_misc")
                ag_toks[wn] = e.emit("gpsimd", lambda g, wn=wn: g.collective_compute(
                    "AllGather", ALU.bypass,
                    replica_groups=[[0, 1, 2, 3, 4, 5, 6, 7]],
                    ins=[ag[wn + "i"]], outs=[ag[wn + "o"]]),
                    waits=[tok_w_done], sig=1)
            tok_ffnw_a_free = e.tok("vector")
            tok_ffnw_b_free = e.tok("vector")

        ps_guard = {}

        def psw(ps):
            t = ps_guard.get(id(ps))
            return [t] if t else []

        def emit_ln(src_bf, srcsq_bf, ncols, row_rs, row_murs, scr1, scr2,
                    extra_waits, sq_cb=None):
            chunks = []
            c0 = 0
            while c0 < ncols:
                cw = min(512, ncols - c0)
                chunks.append((c0, cw)); c0 += cw
            prev_v = None
            prev_tkm = None
            for (c0, cw) in chunks:
                if sq_cb is not None:
                    sq_tiles, sq_tok = sq_cb(c0, cw, prev_tkm)
                else:
                    sq_tiles, sq_tok = srcsq_bf, None
                w0 = list(extra_waits) + ([prev_v] if prev_v else []) + psw(PS4)
                for i in range(NF):
                    e.emit("tensor", lambda t, i=i, c0=c0, cw=cw: t.matmul(
                        PS4[0:1, 0:cw], ONESC[:, :], src_bf[:, i, c0:c0 + cw],
                        start=(i == 0), stop=(i == NF - 1)),
                        waits=(w0 if i == 0 else ()),
                        sig=(1 if i == NF - 1 else 0))
                tkm = None
                for i in range(NF):
                    sqsl = (lambda i_, cc0, ccw: (
                        sq_tiles[:, i_, 0:ccw] if sq_cb is not None
                        else sq_tiles[:, i_, cc0:cc0 + ccw]))(i, c0, cw)
                    tkm = e.emit("tensor", lambda t, i=i, c0=c0, cw=cw,
                                 sqsl=sqsl: t.matmul(
                        PS5[0:1, 0:cw], ONESC[:, :], sqsl,
                        start=(i == 0), stop=(i == NF - 1)),
                        waits=([sq_tok] if (i == 0 and sq_tok) else ()),
                        sig=(1 if i == NF - 1 else 0))
                prev_tkm = tkm
                e.emit("vector", lambda v, c0=c0, cw=cw: v.tensor_scalar(
                    row_murs[0:1, c0:c0 + cw], PS4[0:1, 0:cw], 1.0 / D, None,
                    ALU.mult), waits=[tkm])
                prev_v = e.emit("vector", lambda v, c0=c0, cw=cw:
                                v.tensor_scalar(
                    row_rs[0:1, c0:c0 + cw], PS5[0:1, 0:cw], 1.0 / D, None,
                    ALU.mult))
            ps_guard[id(PS4)] = prev_v
            ps_guard[id(PS5)] = prev_v
            e.emit("vector", lambda v: v.tensor_tensor(
                scr1[0:1, 0:ncols], row_murs[0:1, 0:ncols],
                row_murs[0:1, 0:ncols], ALU.mult))
            e.emit("vector", lambda v: v.scalar_tensor_tensor(
                row_rs[0:1, 0:ncols], scr1[0:1, 0:ncols], -1.0,
                row_rs[0:1, 0:ncols], ALU.mult, ALU.add))
            e.emit("vector", lambda v: v.tensor_scalar(
                row_rs[0:1, 0:ncols], row_rs[0:1, 0:ncols], EPS, None,
                ALU.add))
            e.emit("vector", lambda v: v.reciprocal(scr1[0:1, 0:ncols],
                   row_rs[0:1, 0:ncols]))
            tv = e.tok("vector")
            ts_ = e.emit("scalar", lambda s: s.activation(
                scr2[0:1, 0:ncols], scr1[0:1, 0:ncols], AF.Sqrt), waits=[tv])
            e.emit("vector", lambda v: v.tensor_tensor(
                scr1[0:1, 0:ncols], scr2[0:1, 0:ncols], scr2[0:1, 0:ncols],
                ALU.mult), waits=[ts_])
            e.emit("vector", lambda v: v.tensor_tensor(
                scr1[0:1, 0:ncols], scr1[0:1, 0:ncols], row_rs[0:1, 0:ncols],
                ALU.mult))
            e.emit("vector", lambda v: v.tensor_scalar(
                scr1[0:1, 0:ncols], scr1[0:1, 0:ncols], -0.5, 1.5, ALU.mult,
                ALU.add))
            e.emit("vector", lambda v: v.tensor_tensor(
                row_rs[0:1, 0:ncols], scr2[0:1, 0:ncols], scr1[0:1, 0:ncols],
                ALU.mult))
            e.emit("vector", lambda v: v.tensor_tensor(
                row_murs[0:1, 0:ncols], row_murs[0:1, 0:ncols],
                row_rs[0:1, 0:ncols], ALU.mult))
            return e.tok("vector"), e.tok("tensor")

        def emit_bcast(row, ncols, dst_f32, wait):
            prev = None
            c0 = 0
            while c0 < ncols:
                cw = min(512, ncols - c0)
                w0 = [wait] + ([prev] if prev else []) + psw(PS6)
                tk = e.emit("tensor", lambda t, c0=c0, cw=cw: t.matmul(
                    PS6[:, 0:cw], ONE1F[:, :], row[0:1, c0:c0 + cw],
                    start=True, stop=True), waits=w0, sig=1)
                prev = e.emit("vector", lambda v, c0=c0, cw=cw: v.tensor_copy(
                    dst_f32[:, c0:c0 + cw], PS6[:, 0:cw]), waits=[tk], sig=1)
                c0 += cw
            ps_guard[id(PS6)] = prev
            return prev

        wb_guard = {}

        def emit_proj(wname, rhs_bf, out_cb, extra_mm_waits=(), rhs_cols=T):
            for o in range(NF):
                slot = o % 2
                wg = wb_guard.get(slot)
                tok_l = e.dma("gpsimd", "d_wb%d" % slot, WB[:, slot, :, :],
                              din[wname][o].rearrange("(h p) m -> p h m", p=P),
                              waits=([wg] if wg else []))
                ps = PS01 if (o % 2 == 0) else PS23
                mm_w = [tok_l] + list(extra_mm_waits) + psw(ps)
                nch = (rhs_cols + 511) // 512
                tok_mm = None
                for ci, c0 in enumerate(range(0, rhs_cols, 512)):
                    cw = min(512, rhs_cols - c0)
                    for h in range(NF):
                        tok_mm = e.emit("tensor", lambda t, h=h, c0=c0, cw=cw, ps=ps,
                               slot=slot: t.matmul(
                                   ps[:, c0:c0 + cw], WB[:, slot, h, :],
                                   rhs_bf[:, h, c0:c0 + cw],
                                   start=(h == 0), stop=(h == NF - 1)),
                               waits=(mm_w if (h == 0 and ci == 0) else ()),
                               sig=(1 if (h == NF - 1 and ci == nch - 1) else 0))
                wb_guard[slot] = tok_mm
                out_cb(o, ps, tok_mm)
            return tok_mm

        def emit_tm1m(src_off, waits=()):
            return e.emit("vector", lambda v: v.tensor_scalar(
                VS(130, 146), VS(src_off, src_off + 16), -1.0, 1.0, ALU.mult,
                ALU.add), waits=waits, sig=1)

        def emit_mix(tmoff, waits):
            for i in range(NF):
                e.emit("vector", lambda v, i=i: v.tensor_scalar(
                    B[:, i, 0:T], A[:, i, 1:T + 1], VEC[:, tmoff + i:tmoff + i + 1],
                    None, ALU.mult), waits=(waits if i == 0 else ()), sig=1)
                e.emit("vector", lambda v, i=i: v.scalar_tensor_tensor(
                    B[:, i, 0:T], A[:, i, 0:T], VEC[:, 130 + i:130 + i + 1],
                    B[:, i, 0:T], ALU.mult, ALU.add), sig=1)
            return e.tok("vector")

        tok_xbf = e.dma("gpsimd", "d_xbf", A[:, :, :],
                        din["xT"].rearrange("(i p) t -> p i t", p=P),
                        waits=[w for w in (tok_ld, tok_ffnw_a_free) if w])
        for i in range(NF):
            e.emit("scalar", lambda s, i=i: s.activation(
                B[:, i, :], X[:, i, :], AF.Square),
                waits=([tok_ld, tok_ffnw_b_free] if i == 0 else ()), sig=1)
        tok_sq = e.tok("scalar")

        row_rs0 = CDf[0:1, 0:T + 1]
        row_mu0 = CDf[0:1, 2048:2048 + T + 1]
        scr_r1 = CDf[0:1, 4096:4096 + T + 1]
        scr_r2 = CDf[0:1, 6144:6144 + T + 1]
        ln0_w = [tok_xbf, tok_sq, tok_ones] + ([ag_toks["d1"]] if do_ffnw else [])
        tok_ln0, tok_ln0mm = emit_ln(A, B, T + 1, row_rs0, row_mu0, scr_r1, scr_r2,
                                     ln0_w)
        rsB0 = CDf[:, 8192:8192 + T + 1]
        muB0 = CDf[:, 10240:10240 + T + 1]
        tok_b1 = emit_bcast(row_rs0, T + 1, rsB0, tok_ln0)
        tok_b2 = emit_bcast(row_mu0, T + 1, muB0, tok_ln0)
        for i in range(NF):
            e.emit("vector", lambda v, i=i: v.tensor_tensor(
                Bff[:, 0:T + 1], X[:, i, :], rsB0, ALU.mult),
                waits=([tok_b1, tok_b2, tok_ln0mm] if i == 0 else ()), sig=1)
            e.emit("vector", lambda v, i=i: v.tensor_tensor(
                A[:, i, :], Bff[:, 0:T + 1], muB0, ALU.subtract), sig=1)
        tok_xn0 = e.tok("vector")

        xo_toks = {}
        if stage_ge("l0a"):
            t_tm = emit_tm1m(48, waits=[tok_xn0])
            tok_mixv = emit_mix(48, [t_tm])
            v_toks = {}

            def v_out(o, ps, tok_mm):
                v_toks[o] = e.emit("scalar", lambda s, o=o, ps=ps: s.activation(
                    Ct[:, o, :], ps[:, 0:T], AF.Copy), waits=[tok_mm], sig=1)
                ps_guard[id(ps)] = v_toks[o]
            tok_vproj = emit_proj("wv0", B, v_out, extra_mm_waits=[tok_mixv])

            t_tm = emit_tm1m(32, waits=[tok_mixv])
            tok_mixk = emit_mix(32, [t_tm, tok_vproj])
            kv_toks = {}

            def k_out(o, ps, tok_mm):
                kv_toks[o] = e.emit("vector", lambda v, o=o, ps=ps: v.tensor_tensor(
                    Ct[:, o, :], ps[:, 0:T], Ct[:, o, :], ALU.mult),
                    waits=[tok_mm, v_toks[o]], sig=1)
                ps_guard[id(ps)] = kv_toks[o]
            tok_kproj = emit_proj("wk0", B, k_out, extra_mm_waits=[tok_mixk])

            t_tm = emit_tm1m(64, waits=[tok_mixk])
            tok_mixr = emit_mix(64, [t_tm, tok_kproj])
            r_toks = {}

            def r_out(o, ps, tok_mm):
                r_toks[o] = e.emit("scalar", lambda s, o=o, ps=ps: s.activation(
                    Ct[:, NF + o, :], ps[:, 0:T], AF.Sigmoid), waits=[tok_mm], sig=1)
                ps_guard[id(ps)] = r_toks[o]
            tok_r_lastmm = emit_proj("wr0", B, r_out, extra_mm_waits=[tok_mixr])

            tok_j = e.emit("gpsimd", lambda g: g.iota(
                Af[:, 1, 0:T], [[1, T]], base=0, channel_multiplier=0,
                allow_small_or_imprecise_dtypes=True), waits=[tok_mixr], sig=1)
            for p_ in range(NF):
                St = Af[:, 2, 0:T + 1]
                w_m0 = [kv_toks[p_], tok_mixr]
                if p_ == 0:
                    w_m0 += [tok_w]
                m0 = e.emit("vector", lambda v, St=St: v.memset(St[:, 0:1], 0.0),
                            waits=w_m0, sig=1)
                e.emit("vector", lambda v, p_=p_, St=St: v.tensor_tensor_scan(
                    St[:, 1:T + 1], VEC[:, 96 + p_:97 + p_].broadcast_to([P, T]),
                    Ct[:, p_, :], 0.0, ALU.mult, ALU.add), sig=1)
                e.emit("vector", lambda v, p_=p_, St=St: v.tensor_copy(
                    VEC[:, 114 + p_:115 + p_], St[:, T:T + 1]), sig=1)
                e.emit("vector", lambda v, p_=p_, St=St: v.scalar_tensor_tensor(
                    Af[:, 4, 0:T], Ct[:, p_, :], VEC[:, 16 + p_:17 + p_],
                    St[:, 0:T], ALU.mult, ALU.add), sig=1)
                e.emit("vector", lambda v, p_=p_: v.tensor_tensor(
                    B[:, p_, 0:T], Af[:, 4, 0:T], Ct[:, NF + p_, :], ALU.mult),
                    waits=[r_toks[p_], tok_r_lastmm], sig=1)
            tok_scan = e.tok("vector")
            tok_sbw = e.dma("sync", "d_misc", sb_in, VS(114, 130), waits=[tok_scan])
            tok_ag1 = e.emit("gpsimd", lambda g: g.collective_compute(
                "AllGather", ALU.bypass,
                replica_groups=[[0, 1], [2, 3], [4, 5], [6, 7]],
                ins=[sb_in], outs=[sb_out]), waits=[tok_sbw], sig=1)
            tok_sbl = e.dma("sync", "d_misc", VS(114, 130), sb_out[0:P, :], waits=[tok_ag1])
            prev_cv = None
            for p_ in range(NF):
                sw = e.emit("scalar", lambda s, p_=p_: s.activation(
                    Af[:, 4, 0:T], Af[:, 1, 0:T], AF.Exp,
                    scale=VEC[:, 80 + p_:81 + p_]),
                    waits=[tok_j, tok_sbl, tok_scan, tok_ld, prev_cv], sig=1)
                e.emit("vector", lambda v, p_=p_: v.tensor_scalar(
                    Af[:, 4, 0:T], Af[:, 4, 0:T], VEC[:, 114 + p_:115 + p_],
                    VEC[:, 113:114], ALU.mult, ALU.mult), waits=[sw], sig=1)
                e.emit("vector", lambda v, p_=p_: v.tensor_tensor(
                    Af[:, 4, 0:T], Af[:, 4, 0:T], Ct[:, NF + p_, :], ALU.mult),
                    sig=1)
                prev_cv = e.emit("vector", lambda v, p_=p_: v.tensor_tensor(
                    B[:, p_, 0:T], B[:, p_, 0:T], Af[:, 4, 0:T], ALU.add), sig=1)
            tok_corr = e.tok("vector")

            def wo_out(o, ps, tok_mm):
                xo_toks[o] = e.emit("vector", lambda v, o=o, ps=ps: v.tensor_tensor(
                    X[:, o, 1:T + 1], X[:, o, 1:T + 1], ps[:, 0:T], ALU.add),
                    waits=[tok_mm], sig=1)
                ps_guard[id(ps)] = xo_toks[o]
            tok_woproj = emit_proj("wo0", B, wo_out, extra_mm_waits=[tok_corr])


        Ax = A2[:, 0:16384].rearrange("p (s t) -> p s t", t=T)
        Aq = A2[:, 0:8192].rearrange("p (s t) -> p s t", t=512)
        AU0 = A2[:, 8192:12288].rearrange("p (s t) -> p s t", t=256)
        AU1 = A2[:, 12288:16384].rearrange("p (s t) -> p s t", t=256)
        WBF = WB.rearrange("p a s t -> p (a s t)")[:, 0:4096]\
            .rearrange("p (s t) -> p s t", t=256)
        BxC = B2[:, 0:16384].rearrange("p (s t) -> p s t", t=512)
        c1B = Bff[:, 0:512]; c2B = Bff[:, 512:1024]; invB = Bff[:, 1024:1536]
        zsS = Bff[:, 1536:2048]; gsS = Bff[:, 2048:2560]
        rowA = Bff[0:1, 2560:3072]; rowB = Bff[0:1, 3072:3584]
        rowC = Bff[0:1, 3584:4096]
        MXC = Bff[:, 4096:4104]; MNC = Bff[:, 4104:4112]
        ROW8 = Bff[:, 4112:4240]
        MxT = Bff[:, 4240:4752]; MnT = Bff[:, 4752:5264]
        rs_row = Bff[0:1, 5264:6288]; murs_row = Bff[0:1, 6288:7312]
        TMPB = B2[:, 14624:15136]
        scr1_row = Bff[0:1, 0:1024]; scr2_row = Bff[0:1, 1024:2048]

        fslot_guard = {}

        def ffn_piece_load(slot_i, agname, r0, ag_tok=None):
            sem = "d_fb%d" % slot_i
            dst = (WBF, AU0, AU1)[slot_i]
            wg = fslot_guard.get(slot_i)
            src = ag[agname][r0:r0 + 32, :].rearrange("h (g c) -> (h g) c", c=256)\
                .rearrange("(h p) c -> p h c", p=P)
            return e.dma("sync", sem, dst[:, :, :], src,
                         waits=([wg] if wg else [])
                         + ([ag_tok] if ag_tok else []))

        def emit_row_gather(src_tokmaj, n8, dst_row, wait):
            tk = e.emit("tensor", lambda t, n8=n8: t.transpose(
                PS4[0:n8, 0:P], src_tokmaj, IDF[:, :]),
                waits=[wait] + psw(PS4), sig=1)
            tv = e.emit("vector", lambda v, n8=n8: v.tensor_copy(
                ROW8[0:n8, 0:P], PS4[0:n8, 0:P]), waits=[tk])
            ps_guard[id(PS4)] = tv
            return e.dma("sync", "d_misc", dst_row[0:1, 0:n8 * P],
                         ROW8[0:n8, 0:P], waits=[tv])

        def emit_ffn(layer, tok_xready, tok_afree, tok_bfree):
            gi, ui, di = (0, 1, 2) if layer == 0 else (3, 4, 5)
            agg, agu, agd = [wnames[i] + "o" for i in (gi, ui, di)]
            mgv = VS(154 + gi, 155 + gi); muv = VS(154 + ui, 155 + ui)
            mdv = VS(154 + di, 155 + di)
            t_xb = None
            for i in range(NF):
                t_xb = e.emit("scalar", lambda s, i=i: s.activation(
                    Ax[:, i, :], X[:, i, 1:T + 1], AF.Copy),
                    waits=([tok_xready, tok_afree] if i == 0 else ()))

            def sq_cb(c0, cw, prev_tkm):
                tok = None
                for i in range(NF):
                    w0 = []
                    if i == 0:
                        w0 = [tok_bfree] if c0 == 0 else []
                        if prev_tkm is not None:
                            w0 = w0 + [prev_tkm]
                    tok = e.emit("scalar", lambda s, i=i, c0=c0, cw=cw:
                                 s.activation(
                        BxC[:, i, 0:cw], X[:, i, 1 + c0:1 + c0 + cw],
                        AF.Square), waits=w0)
                return BxC, tok

            tok_lnf, tok_lnfmm = emit_ln(Ax, None, T, rs_row, murs_row,
                                         scr1_row, scr2_row, [t_xb],
                                         sq_cb=sq_cb)
            tok_x_final = {}
            ld_idx = [0]
            tok_dn_prev = None
            for c in range(2):
                cs = 1 + c * 512
                tv = e.emit("vector", lambda v, cs=cs: v.tensor_tensor(
                    MxT[:, :], X[:, 0, cs:cs + 512], X[:, 1, cs:cs + 512],
                    ALU.max), waits=[tok_lnf, tok_dn_prev])
                tv = e.emit("vector", lambda v, cs=cs: v.tensor_tensor(
                    MnT[:, :], X[:, 0, cs:cs + 512], X[:, 1, cs:cs + 512],
                    ALU.min))
                for i in range(2, NF):
                    tv = e.emit("vector", lambda v, i=i, cs=cs: v.tensor_tensor(
                        MxT[:, :], MxT[:, :], X[:, i, cs:cs + 512], ALU.max))
                    tv = e.emit("vector", lambda v, i=i, cs=cs: v.tensor_tensor(
                        MnT[:, :], MnT[:, :], X[:, i, cs:cs + 512], ALU.min))
                for c4 in range(4):
                    tkx = e.emit("tensor", lambda t, c4=c4: t.transpose(
                        PS5[:, 0:P], MxT[:, c4 * P:(c4 + 1) * P], IDF[:, :]),
                        waits=[tv] + psw(PS5), sig=1)
                    tvx = e.emit("vector", lambda v, c4=c4: v.reduce_max(
                        MXC[:, c4:c4 + 1], PS5[:, 0:P], AX.X), waits=[tkx])
                    tkn = e.emit("tensor", lambda t, c4=c4: t.transpose(
                        PS5[:, 0:P], MnT[:, c4 * P:(c4 + 1) * P], IDF[:, :]),
                        waits=[tvx], sig=1)
                    tv = e.emit("vector", lambda v, c4=c4: v.tensor_reduce(
                        MNC[:, c4:c4 + 1], PS5[:, 0:P], AX.X, ALU.min),
                        waits=[tkn])
                ps_guard[id(PS5)] = tv
                t_mx = emit_row_gather(MXC[:, 0:4], 4, rowA, tv)
                t_mn = emit_row_gather(MNC[:, 0:4], 4, rowB, t_mx)
                rsc = rs_row[0:1, c * 512:(c + 1) * 512]
                mursc = murs_row[0:1, c * 512:(c + 1) * 512]
                tv = e.emit("vector", lambda v, rsc=rsc: v.reciprocal(
                    rowC, rsc), waits=[tok_lnf, t_mx, t_mn])
                tv = e.emit("vector", lambda v, mursc=mursc: v.tensor_tensor(
                    rowC, rowC, mursc, ALU.mult))
                tv = e.emit("vector", lambda v: v.tensor_tensor(
                    rowA, rowA, rowC, ALU.subtract))
                tv = e.emit("vector", lambda v: v.tensor_tensor(
                    rowB, rowC, rowB, ALU.subtract))
                tv = e.emit("vector", lambda v: v.tensor_tensor(
                    rowA, rowA, rowB, ALU.max))
                tv = e.emit("vector", lambda v, rsc=rsc: v.tensor_tensor(
                    rowA, rowA, rsc, ALU.mult))
                tv = e.emit("vector", lambda v: v.tensor_single_scalar(
                    rowA, rowA, EPS, ALU.max))
                tv = e.emit("vector", lambda v: v.tensor_single_scalar(
                    rowC, rowA, 1.0 / 128.0, ALU.mult))
                tv = e.emit("vector", lambda v: v.reciprocal(rowB, rowC))
                tv = e.emit("vector", lambda v, rsc=rsc: v.tensor_tensor(
                    rowA, rsc, rowB, ALU.mult))
                tok_rows = e.emit("vector", lambda v, mursc=mursc:
                                  v.tensor_tensor(
                    rowB, mursc, rowB, ALU.mult))
                tb = emit_bcast(rowA, 512, c1B, tok_rows)
                tb2 = emit_bcast(rowB, 512, c2B, tb)
                tb3 = emit_bcast(rowC, 512, invB, tb2)
                prev_j = None
                tok_q = None
                for i in range(NF):
                    w0 = [tb3, tok_lnfmm]
                    e.emit("vector", lambda v, i=i, cs=cs: v.tensor_tensor(
                        zsS[:, :], X[:, i, cs:cs + 512], c1B, ALU.mult),
                        waits=(w0 if i == 0 else ()))
                    tz = e.emit("vector", lambda v: v.tensor_tensor(
                        zsS[:, :], zsS[:, :], c2B, ALU.subtract))
                    ts_ = e.emit("scalar", lambda s: s.activation(
                        I32S[:, :], zsS[:, :], AF.Copy, bias=0.5),
                        waits=[tz, prev_j])
                    e.emit("vector", lambda v: v.scalar_tensor_tensor(
                        gsS[:, :], I32S[:, :], -0.5, zsS[:, :], ALU.add,
                        ALU.is_gt), waits=[ts_])
                    prev_j = e.emit("vector", lambda v: v.scalar_tensor_tensor(
                        gsS[:, :], gsS[:, :], -1.0, I32S[:, :], ALU.mult,
                        ALU.add))
                    tok_q = e.emit("vector", lambda v, i=i:
                                   v.scalar_tensor_tensor(
                        Aq[:, i, :], gsS[:, :], 127.0, invB, ALU.min,
                        ALU.mult))
                z_toks = {}
                for db in range(32):
                    rank, dbl = db // 4, db % 4
                    sl_g = ld_idx[0] % 3; ld_idx[0] += 1
                    sl_u = ld_idx[0] % 3; ld_idx[0] += 1
                    tlg = ffn_piece_load(sl_g, agg, rank * P + dbl * 32,
                                         ag_toks[wnames[gi]])
                    tlu = ffn_piece_load(sl_u, agu, rank * P + dbl * 32,
                                         ag_toks[wnames[ui]])
                    slg = (WBF, AU0, AU1)[sl_g]; slu = (WBF, AU0, AU1)[sl_u]
                    for half in range(2):
                        psG = PS01[:, half * 512:(half + 1) * 512]
                        psU = PS23[:, half * 512:(half + 1) * 512]
                        for h in range(NF):
                            e.emit("tensor", lambda t, h=h, half=half, slg=slg,
                                   psG=psG: t.matmul(
                                       psG, slg[:, h, half * P:(half + 1) * P],
                                       Aq[:, h, :], start=(h == 0),
                                       stop=(h == NF - 1)),
                                   waits=(([tlg, tok_q, ag_toks[wnames[gi]]]
                                           + psw(PS01)) if h == 0 else ()))
                        for h in range(NF):
                            e.emit("tensor", lambda t, h=h, half=half, slu=slu,
                                   psU=psU: t.matmul(
                                       psU, slu[:, h, half * P:(half + 1) * P],
                                       Aq[:, h, :], start=(h == 0),
                                       stop=(h == NF - 1)),
                                   waits=(([tlu, ag_toks[wnames[ui]]]
                                           + psw(PS23)) if h == 0 else ()),
                                   sig=(1 if h == NF - 1 else 0))
                    tok_gu = e.tok("tensor")
                    for half in range(2):
                        o = db * 2 + half
                        psG = PS01[:, half * 512:(half + 1) * 512]
                        psU = PS23[:, half * 512:(half + 1) * 512]
                        tss = e.emit("scalar", lambda s, psG=psG: s.activation(
                            zsS[:, :], psG, AF.Silu, scale=mgv),
                            waits=[tok_gu, z_toks.get(o - 1)])
                        z_toks[o] = e.emit("vector", lambda v, o=o, psU=psU:
                                           v.scalar_tensor_tensor(
                            Zt[:, o, :], psU, muv, zsS[:, :], ALU.mult,
                            ALU.mult), waits=[tss])
                    fslot_guard[sl_g] = z_toks[db * 2 + 1]
                    fslot_guard[sl_u] = z_toks[db * 2 + 1]
                    ps_guard[id(PS01)] = z_toks[db * 2 + 1]
                    ps_guard[id(PS23)] = z_toks[db * 2 + 1]
                sa_toks = {}
                for m in range(32):
                    a, b = 2 * m, 2 * m + 1
                    ps_a = PS01[:, 0:512]; ps_b = PS01[:, 512:1024]
                    w0 = [z_toks[a], z_toks[b], tok_gld] + psw(PS01)
                    e.emit("tensor", lambda t, a=a: t.matmul(
                        PS01[:, 0:512], HAD[:, 0, :], Zt[:, a, :], start=True,
                        stop=False), waits=w0)
                    e.emit("tensor", lambda t, b=b: t.matmul(
                        PS01[:, 0:512], HAD[:, 0, :], Zt[:, b, :], start=False,
                        stop=True))
                    e.emit("tensor", lambda t, a=a: t.matmul(
                        PS01[:, 512:1024], HAD[:, 0, :], Zt[:, a, :],
                        start=True, stop=False))
                    tmm = e.emit("tensor", lambda t, b=b: t.matmul(
                        PS01[:, 512:1024], HAD[:, 1, :], Zt[:, b, :],
                        start=False, stop=True), sig=1)
                    if m % 2 == 0:
                        e.emit("scalar", lambda s, a=a: s.activation(
                            Zt[:, a, :], PS01[:, 0:512], AF.Copy), waits=[tmm])
                        sa_toks[m] = e.emit("scalar", lambda s, b=b:
                                            s.activation(
                            Zt[:, b, :], PS01[:, 512:1024], AF.Copy))
                    else:
                        e.emit("vector", lambda v, a=a: v.tensor_copy(
                            Zt[:, a, :], PS01[:, 0:512]), waits=[tmm])
                        sa_toks[m] = e.emit("vector", lambda v, b=b:
                                            v.tensor_copy(
                            Zt[:, b, :], PS01[:, 512:1024]))
                    ps_guard[id(PS01)] = sa_toks[m]
                tok_sa_s = e.tok("scalar"); tok_sa_v = e.tok("vector")
                spare = 64
                phys = {i: i for i in range(64)}

                def zslot(sl):
                    return TMPB[:, :] if sl == 64 else Zt[:, sl, :]
                tbf = e.emit("vector", lambda v: v.memset(TMPB[:, :], 0.0),
                             waits=[tok_sa_s, tok_sa_v])
                for st_ in range(1, 6):
                    bit = 1 << st_
                    for a in range(64):
                        if a & bit:
                            continue
                        b = a | bit
                        pa, pb, sp = phys[a], phys[b], spare
                        e.emit("vector", lambda v, pa=pa, pb=pb, sp=sp:
                               v.tensor_tensor(
                            zslot(sp), zslot(pa), zslot(pb), ALU.add))
                        tbf = e.emit("vector", lambda v, pa=pa, pb=pb:
                                     v.tensor_tensor(
                            zslot(pb), zslot(pa), zslot(pb), ALU.subtract))
                        phys[a], spare = sp, pa
                tok_bf = tbf
                tv2 = e.emit("vector", lambda v, p0=phys[0], p1=phys[1]:
                             v.tensor_tensor(
                    MxT[:, :], zslot(p0), zslot(p1), ALU.max),
                    waits=[tok_bf])
                tv2 = e.emit("vector", lambda v, p0=phys[0], p1=phys[1]:
                             v.tensor_tensor(
                    MnT[:, :], zslot(p0), zslot(p1), ALU.min))
                for i in range(2, 64):
                    tv2 = e.emit("vector", lambda v, pi_=phys[i]:
                                 v.tensor_tensor(
                        MxT[:, :], MxT[:, :], zslot(pi_), ALU.max))
                    tv2 = e.emit("vector", lambda v, pi_=phys[i]:
                                 v.tensor_tensor(
                        MnT[:, :], MnT[:, :], zslot(pi_), ALU.min))
                for c4 in range(4):
                    tkx = e.emit("tensor", lambda t, c4=c4: t.transpose(
                        PS5[:, 0:P], MxT[:, c4 * P:(c4 + 1) * P], IDF[:, :]),
                        waits=[tv2] + psw(PS5), sig=1)
                    tvx = e.emit("vector", lambda v, c4=c4: v.reduce_max(
                        MXC[:, c4:c4 + 1], PS5[:, 0:P], AX.X), waits=[tkx])
                    tkn = e.emit("tensor", lambda t, c4=c4: t.transpose(
                        PS5[:, 0:P], MnT[:, c4 * P:(c4 + 1) * P], IDF[:, :]),
                        waits=[tvx], sig=1)
                    tv2 = e.emit("vector", lambda v, c4=c4: v.tensor_reduce(
                        MNC[:, c4:c4 + 1], PS5[:, 0:P], AX.X, ALU.min),
                        waits=[tkn])
                ps_guard[id(PS5)] = tv2
                tok_mz = emit_row_gather(MXC[:, 0:4], 4, rowA, tv2)
                tok_mn2 = emit_row_gather(MNC[:, 0:4], 4, rowB, tok_mz)
                tv2 = e.emit("vector", lambda v: v.tensor_scalar(
                    rowB, rowB, -1.0, None, ALU.mult), waits=[tok_mn2])
                tok_mz = e.emit("vector", lambda v: v.tensor_tensor(
                    rowA, rowA, rowB, ALU.max))
                tv2 = e.emit("vector", lambda v: v.tensor_scalar(
                    rowA, rowA, FW_NORM, EPS, ALU.mult, ALU.max),
                    waits=[tok_mz])
                tv2 = e.emit("vector", lambda v: v.tensor_single_scalar(
                    rowC, rowA, 1.0 / 128.0, ALU.mult))
                tv2 = e.emit("vector", lambda v: v.reciprocal(rowB, rowA))
                tv2 = e.emit("vector", lambda v: v.tensor_single_scalar(
                    rowB, rowB, 128.0 * FW_NORM, ALU.mult))
                tzb = emit_bcast(rowB, 512, c1B, tv2)
                tzb2 = emit_bcast(rowC, 512, c2B, tzb)
                prev_j = None
                tok_zq = None
                for i in range(64):
                    pz = phys[i]
                    e.emit("vector", lambda v, pz=pz: v.tensor_tensor(
                        zsS[:, :], zslot(pz), c1B, ALU.mult),
                        waits=([tzb, tzb2] if i == 0 else ()))
                    tz = e.tok("vector")
                    ts_ = e.emit("scalar", lambda s: s.activation(
                        I32S[:, :], zsS[:, :], AF.Copy, bias=0.5),
                        waits=[tz, prev_j])
                    e.emit("vector", lambda v: v.scalar_tensor_tensor(
                        gsS[:, :], I32S[:, :], -0.5, zsS[:, :], ALU.add,
                        ALU.is_gt), waits=[ts_])
                    prev_j = e.emit("vector", lambda v: v.scalar_tensor_tensor(
                        gsS[:, :], gsS[:, :], -1.0, I32S[:, :], ALU.mult,
                        ALU.add))
                    tok_zq = e.emit("vector", lambda v, pz=pz:
                                    v.scalar_tensor_tensor(
                        zslot(pz), gsS[:, :], 127.0, c2B, ALU.min, ALU.mult))
                for h2 in range(8):
                    last_pc_tok = None
                    for pc in range(4):
                        sl = ld_idx[0] % 3; ld_idx[0] += 1
                        tp = ffn_piece_load(sl, agd, h2 * P + pc * 32,
                                            ag_toks[wnames[di]])
                        slw = (WBF, AU0, AU1)[sl]
                        for half in range(2):
                            psum = (PS01 if half == 0 else PS23)[:, 0:512]
                            for i16 in range(16):
                                i_log = pc * 16 + i16
                                last_pc_tok = e.emit(
                                    "tensor", lambda t, slw=slw, i16=i16,
                                    half=half, psum=psum,
                                    st_=(pc == 0 and i16 == 0),
                                    sp_=(pc == 3 and i16 == 15),
                                    pz=phys[i_log]: t.matmul(
                                        psum,
                                        slw[:, i16, half * P:(half + 1) * P],
                                        zslot(pz),
                                        start=st_, stop=sp_),
                                    waits=(([tp, tok_zq,
                                             ag_toks[wnames[di]]] +
                                            (psw(PS01) + psw(PS23)
                                             if (pc == 0 and half == 0)
                                             else []))
                                           if i16 == 0 else ()),
                                    sig=(1 if i16 == 15 else 0))
                        fslot_guard[sl] = last_pc_tok
                    tmm = last_pc_tok
                    for half in range(2):
                        h_out = h2 * 2 + half
                        psum = (PS01 if half == 0 else PS23)[:, 0:512]
                        tok_x_final[(c, h_out)] = e.emit(
                            "vector", lambda v, h_out=h_out, psum=psum, cs=cs:
                            v.scalar_tensor_tensor(
                                X[:, h_out, cs:cs + 512], psum, mdv,
                                X[:, h_out, cs:cs + 512], ALU.mult, ALU.add),
                            waits=[tmm])
                    ps_guard[id(PS01)] = tok_x_final[(c, h2 * 2 + 1)]
                    ps_guard[id(PS23)] = tok_x_final[(c, h2 * 2 + 1)]
                tok_dn_prev = tok_x_final[(c, 15)]
            return e.tok("vector"), e.tok("tensor")

        if stage_ge("l0f"):
            fslot_guard[0] = tok_woproj
            fslot_guard[1] = tok_corr
            fslot_guard[2] = tok_corr
            tokf_v, tokf_t = emit_ffn(0, xo_toks[NF - 1], tok_corr, tok_woproj)
            wb_guard[0] = tokf_t
            wb_guard[1] = tokf_t
        if STAGE == "l0f":
            fins = [e.dma("sync", "d_out", y[i], X[:, i, 1:T + 1],
                          waits=[tokf_v]) for i in range(NF)]
            e.emit("sync", lambda s: s.nop(), waits=[fins[-1]])
            em.build(sems, block)
            return nc
        if STAGE == "l0a":
            fins = [e.dma("sync", "d_out", y[i], X[:, i, 1:T + 1], waits=[xo_toks[i]])
                    for i in range(NF)]
            e.emit("sync", lambda s: s.nop(), waits=[fins[-1]])
            em.build(sems, block)
            return nc

        VG = A2full.rearrange("p (h k c) -> p h k c", k=8, c=65)
        RT = A2full[:, 0:4096].rearrange("p (s t) -> p s t", t=T)
        Bv = B2[:, 0:16384].rearrange("p (s t) -> p s t", t=T)
        AVB = I32S.bitcast(BF16)
        DRow = Bff[0:1, 2816:3328]

        t_xb1 = None
        for i in range(NF):
            t_xb1 = e.emit("scalar", lambda s, i=i: s.activation(
                Ax[:, i, :], X[:, i, 1:T + 1], AF.Copy),
                waits=([tokf_v, tokf_t] if i == 0 else ()))

        def sq_cb1(c0, cw, prev_tkm):
            tok = None
            for i in range(NF):
                w0 = [prev_tkm] if (i == 0 and prev_tkm is not None) else []
                tok = e.emit("scalar", lambda s, i=i, c0=c0, cw=cw: s.activation(
                    BxC[:, i, 0:cw], X[:, i, 1 + c0:1 + c0 + cw], AF.Square),
                    waits=w0)
            return BxC, tok

        tok_ln1, tok_ln1mm = emit_ln(Ax, None, T, rs_row, murs_row,
                                     scr1_row, scr2_row, [t_xb1], sq_cb=sq_cb1)
        rsB1 = CDf[:, 8192:9216]
        muB1 = CDf[:, 9216:10240]
        tok_b1 = emit_bcast(rs_row, T, rsB1, tok_ln1)
        tok_b2 = emit_bcast(murs_row, T, muB1, tok_b1)
        for i in range(NF):
            e.emit("vector", lambda v, i=i: v.tensor_tensor(
                Bff[:, 0:T], X[:, i, 1:T + 1], rsB1, ALU.mult),
                waits=([tok_b1, tok_b2, tok_ln1mm] if i == 0 else ()), sig=1)
            e.emit("vector", lambda v, i=i: v.tensor_tensor(
                Ax[:, i, :], Bff[:, 0:T], muB1, ALU.subtract), sig=1)
        tok_xn1 = e.tok("vector")

        def stage_dump(tiles, tok, stg=None):
            stg = Bff[:, 0:1024] if stg is None else stg
            last = tok
            for i, tl in enumerate(tiles):
                cv = e.emit("vector", lambda v, tl=tl, stg=stg: v.tensor_copy(
                    stg, tl), waits=[last], sig=1)
                last = e.dma("sync", "d_misc", y[i], stg, waits=[cv])
            e.emit("sync", lambda s: s.nop(), waits=[last])
            em.build(sems, block)
            return nc

        if STAGE == "dn":
            return stage_dump([Ax[:, i, :] for i in range(NF)], tok_xn1)

        q_toks1 = {}; k_toks1 = {}; v_toks1 = {}

        def q1_out(o, ps, tok_mm):
            q_toks1[o] = e.emit("scalar", lambda s, o=o, ps=ps: s.activation(
                Ct[:, o, :], ps[:, 0:T], AF.Identity, bias=BQ[:, o:o + 1]),
                waits=[tok_mm, tok_bq], sig=1)
            ps_guard[id(ps)] = q_toks1[o]
        tok_qproj = emit_proj("wq1", Ax, q1_out, extra_mm_waits=[tok_xn1])

        if STAGE == "dq0":
            return stage_dump([Ct[:, o, :] for o in range(NF)],
                              q_toks1[NF - 1])

        def k1_out(o, ps, tok_mm):
            k_toks1[o] = e.emit("scalar", lambda s, o=o, ps=ps: s.activation(
                Ct[:, 16 + o, :], ps[:, 0:T], AF.Identity,
                bias=BQ[:, 16 + o:17 + o]), waits=[tok_mm], sig=1)
            ps_guard[id(ps)] = k_toks1[o]
        tok_kproj = emit_proj("wk1", Ax, k1_out, extra_mm_waits=[tok_qproj])

        def v1_out(o, ps, tok_mm):
            v_toks1[o] = e.emit("scalar", lambda s, o=o, ps=ps: s.activation(
                Bv[:, o, :], ps[:, 0:T], AF.Identity,
                bias=BQ[:, 32 + o:33 + o]), waits=[tok_mm], sig=1)
            ps_guard[id(ps)] = v_toks1[o]
        tok_vproj = emit_proj("wv1", Ax, v1_out, extra_mm_waits=[tok_kproj])

        tok_rt = e.dma("gpsimd", "d_xbf", A2full[:, 0:4096],
                       din["ropes"].rearrange("p s t -> p (s t)"),
                       waits=[tok_vproj])

        rope_v2 = None
        tok_rope_done = None
        for oo in range(32):
            ci, si = (0, 1) if oo < 16 else (2, 3)
            ev_tok = q_toks1[oo] if oo < 16 else k_toks1[oo - 16]
            w0 = [tok_rt, ev_tok] + psw(PS01)
            e.emit("tensor", lambda t, oo=oo: t.matmul(
                PS01[:, 0:512], PERM[:, :], Ct[:, oo, 0:512],
                start=True, stop=True), waits=w0)
            m2 = e.emit("tensor", lambda t, oo=oo: t.matmul(
                PS01[:, 512:1024], PERM[:, :], Ct[:, oo, 512:1024],
                start=True, stop=True), sig=1)
            e.emit("vector", lambda v, si=si: v.tensor_tensor(
                AVB[:, 0:512], PS01[:, 0:512], RT[:, si, 0:512], ALU.mult),
                waits=[m2], sig=1)
            rope_v2 = e.emit("vector", lambda v, si=si: v.tensor_tensor(
                AVB[:, 512:1024], PS01[:, 512:1024], RT[:, si, 512:1024],
                ALU.mult), sig=1)
            e.emit("vector", lambda v, oo=oo, ci=ci: v.tensor_tensor(
                Ct[:, oo, :], Ct[:, oo, :], RT[:, ci, :], ALU.mult), sig=1)
            tok_rope_done = e.emit("vector", lambda v, oo=oo: v.tensor_tensor(
                Ct[:, oo, :], Ct[:, oo, :], AVB[:, 0:1024], ALU.add), sig=1)
            ps_guard[id(PS01)] = rope_v2

        def stage_cut(tok):
            fins = [e.dma("sync", "d_out", y[i], X[:, i, 1:T + 1], waits=[tok])
                    for i in range(NF)]
            e.emit("sync", lambda s: s.nop(), waits=[fins[-1]])
            em.build(sems, block)
            return nc

        if STAGE == "s3":
            return stage_cut(tok_rope_done)
        if STAGE == "dq":
            return stage_dump([Ct[:, o, :] for o in range(NF)], tok_rope_done)
        if STAGE == "dk":
            return stage_dump([Ct[:, NF + o, :] for o in range(NF)],
                              tok_rope_done)
        if STAGE == "dv":
            return stage_dump([Bv[:, o, :] for o in range(NF)], tok_rope_done,
                              stg=Af[:, 4, 0:T])

        KSUB = os.environ.get("KSUB", "")
        if KSUB:
            nsl = {"t0": 1, "t1": 2, "tall": 16}[KSUB]
            tr_tok = None
            for sub in range(2):
                for kt in range(8):
                    slot = sub * 8 + kt
                    if slot >= nsl:
                        continue
                    tr_tok = e.emit("tensor", lambda t, sub=sub, kt=kt,
                                    slot=slot: t.transpose(
                        PSB[:, slot * 64:(slot + 1) * 64],
                        Bv[sub * 64:sub * 64 + 64, 0, kt * 128:(kt + 1) * 128],
                        IDB[sub * 64:sub * 64 + 64, sub * 64:sub * 64 + 64]),
                        waits=[v_toks1[0]], sig=1)
            cp = e.emit("vector", lambda v: v.tensor_copy(
                VG[:, 0, 0, 0:64], PSB[:, 0:64]), waits=[tr_tok], sig=1)
            return stage_cut(cp)
        psb_guard = None
        tok_vg = None
        for o in range(16):
            tr_last = None
            for kt in range(8):
                w0 = ([v_toks1[o]] + ([psb_guard] if psb_guard else [])) \
                    if kt == 0 else []
                tr_last = e.emit("tensor", lambda t, o=o, kt=kt: t.transpose(
                    PSB[:, kt * 128:(kt + 1) * 128],
                    Bv[:, o, kt * 128:(kt + 1) * 128], IDB[:, :]),
                    waits=w0, sig=(1 if kt == 7 else 0))
            for sub in range(2):
                h = 2 * o + sub
                for kt in range(8):
                    w0 = [tr_last]
                    if o == 0 and sub == 0 and kt == 0:
                        w0.append(tok_rope_done)
                    tok_vg = e.emit("vector", lambda v, h=h, kt=kt, sub=sub:
                                    v.tensor_copy(VG[:, h, kt, 0:64],
                                                  PSB[:, kt * 128 + sub * 64:
                                                      kt * 128 + sub * 64 + 64]),
                                    waits=w0, sig=1)
                tok_vg = e.emit("vector", lambda v, h=h: v.memset(
                    VG[:, h, :, 64:65], 1.0), sig=1)
            psb_guard = tok_vg

        if STAGE == "s4":
            return stage_cut(tok_vg)

        tok_kw = e.dma("sync", "d_misc", agk_in.rearrange("(i p) t -> p i t", p=P),
                       Ct[:, 16:32, :], waits=[tok_rope_done])
        tok_vw = e.dma("sync", "d_misc", agv_in, A2full[:, :], waits=[tok_vg])
        tok_agk = e.emit("gpsimd", lambda g: g.collective_compute(
            "AllGather", ALU.bypass, replica_groups=[[0, 1], [2, 3], [4, 5], [6, 7]],
            ins=[agk_in], outs=[agk_out]), waits=[tok_kw], sig=1)
        tok_agv = e.emit("gpsimd", lambda g: g.collective_compute(
            "AllGather", ALU.bypass, replica_groups=[[0, 1], [2, 3], [4, 5], [6, 7]],
            ins=[agv_in], outs=[agv_out]), waits=[tok_vw], sig=1)

        if STAGE == "s5":
            t1 = e.dma("sync", "d_misc", B2[:, 0:1024],
                       agk_out[0:P, 0:1024], waits=[tok_agk])
            t2 = e.dma("sync", "d_misc", B2[:, 1024:2048],
                       agv_out[0:P, 0:1024], waits=[tok_agv, t1])
            return stage_cut(t2)

        KPc = [0, 1024]; VPc = [2048, 2568]; ESCc = [3088, 3600]
        kp_tok = {}; vp_tok = {}; kp_guard = {}; vp_guard = {}

        def prefetch(h):
            slot = h % 2
            sub = h % 2
            w1 = [tok_agk] + ([kp_guard[slot]] if slot in kp_guard else []) \
                + ([tok_vg] if h < 2 else [])
            kp_tok[h] = e.dma("sync", "d_kb%d" % slot,
                              B2[sub * 64:sub * 64 + 64, KPc[slot]:KPc[slot] + 1024],
                              agk_out[h * 64:(h + 1) * 64, :], waits=w1)
            w2 = [tok_agv] + ([vp_guard[slot]] if slot in vp_guard else []) \
                + ([tok_vg] if h < 2 else [])
            vp_tok[h] = e.dma("sync", "d_kb%d" % slot,
                              B2[:, VPc[slot]:VPc[slot] + 520],
                              agv_out[0:P, h * 520:(h + 1) * 520], waits=w2)
        prefetch(0)
        prefetch(1)
        esc_guard = {}
        avs_guard = None
        att_last = None
        first_exp = [True]
        for h in range(32):
            o, sub, slot = h // 2, h % 2, h % 2
            for c in range(2):
                own_max = 4 * c + 4
                kts = [("p", j) for j in range(8)] + \
                      [("o", j) for j in range(own_max)]
                n_kt = len(kts)
                av = None
                last_prev_mm = None
                for idx, (kind, j) in enumerate(kts):
                    sps = PS4 if idx % 2 == 0 else PS5
                    es = idx % 2
                    if kind == "p":
                        lhs = B2[sub * 64:sub * 64 + 64,
                                 KPc[slot] + j * 128:KPc[slot] + (j + 1) * 128]
                        vlhs = B2[:, VPc[slot] + j * 65:VPc[slot] + (j + 1) * 65]
                    else:
                        lhs = Ct[sub * 64:sub * 64 + 64, 16 + o,
                                 j * 128:(j + 1) * 128]
                        vlhs = VG[:, h, j, :]
                    mm_w = psw(sps) + ([kp_tok[h]] if idx == 0 else [])
                    mm = e.emit("tensor", lambda t, lhs=lhs, sps=sps, o=o,
                                sub=sub, c=c: t.matmul(
                        sps[:, 0:512], lhs,
                        Ct[sub * 64:sub * 64 + 64, o, c * 512:(c + 1) * 512],
                        start=True, stop=True), waits=mm_w, sig=1)
                    if kind == "p":
                        last_prev_mm = mm
                    ex_w = [mm] + ([esc_guard[es]] if es in esc_guard else [])
                    if first_exp[0]:
                        ex_w.append(tok_vg)
                        first_exp[0] = False
                    ESCa = B2[:, ESCc[es]:ESCc[es] + 512]
                    if kind == "p":
                        ex = e.emit("scalar", lambda s, sps=sps, ESCa=ESCa:
                                    s.activation(ESCa, sps[:, 0:512], AF.Exp,
                                                 bias=VEC[:, 112:113]),
                                    waits=ex_w, sig=1)
                    else:
                        ex = e.emit("scalar", lambda s, sps=sps, ESCa=ESCa:
                                    s.activation(ESCa, sps[:, 0:512], AF.Exp,
                                                 bias=np.float32(-SMC)),
                                    waits=ex_w, sig=1)
                    ps_guard[id(sps)] = ex
                    mask_tok = ex
                    if kind == "o":
                        loc = j - 4 * c
                        if loc > 0:
                            mask_tok = e.emit("vector", lambda v, ESCa=ESCa,
                                              loc=loc: v.memset(
                                ESCa[:, 0:loc * 128], 0.0), waits=[ex], sig=1)
                        if 0 <= loc < 4:
                            mask_tok = e.emit("vector", lambda v, ESCa=ESCa,
                                              loc=loc: v.tensor_tensor(
                                ESCa[:, loc * 128:(loc + 1) * 128],
                                ESCa[:, loc * 128:(loc + 1) * 128],
                                TRI[:, :], ALU.mult), waits=[ex], sig=1)
                    av_w = [mask_tok] + ((psw(PS6) + [vp_tok[h]])
                                         if idx == 0 else [])
                    av = e.emit("tensor", lambda t, vlhs=vlhs, ESCa=ESCa,
                                st_=(idx == 0), sp_=(idx == n_kt - 1): t.matmul(
                        PS6[0:65, 0:512], vlhs, ESCa, start=st_, stop=sp_),
                        waits=av_w, sig=1)
                    esc_guard[es] = av
                    if kind == "p" and j == 7:
                        vp_guard[slot] = av
                kp_guard[slot] = last_prev_mm
                rc = e.emit("vector", lambda v: v.reciprocal(
                    DRow[0:1, 0:512], PS6[64:65, 0:512]), waits=[av], sig=1)
                bc = e.emit("tensor", lambda t: t.matmul(
                    PS01[0:64, 0:512], ONE1F[0:1, 0:64], DRow[0:1, 0:512],
                    start=True, stop=True), waits=[rc] + psw(PS01), sig=1)
                ev = e.emit("scalar", lambda s: s.activation(
                    AVB[0:64, 0:512], PS6[0:64, 0:512], AF.Copy),
                    waits=[av] + ([avs_guard] if avs_guard else []), sig=1)
                att_last = e.emit("vector", lambda v, o=o, sub=sub, c=c:
                                  v.tensor_tensor(
                    Ct[sub * 64:sub * 64 + 64, o, c * 512:(c + 1) * 512],
                    AVB[0:64, 0:512], PS01[0:64, 0:512], ALU.mult),
                    waits=[ev, bc], sig=1)
                ps_guard[id(PS6)] = att_last
                ps_guard[id(PS01)] = att_last
                avs_guard = att_last
            if h + 2 < 32:
                prefetch(h + 2)

        if STAGE == "s6":
            return stage_cut(att_last)
        if STAGE == "datt":
            return stage_dump([Ct[:, o, :] for o in range(NF)], att_last)

        xo1_toks = {}

        def wo1_out(o, ps, tok_mm):
            xo1_toks[o] = e.emit("vector", lambda v, o=o, ps=ps:
                                 v.scalar_tensor_tensor(
                X[:, o, 1:T + 1], ps[:, 0:T], BQ[:, 48 + o:49 + o],
                X[:, o, 1:T + 1], ALU.add, ALU.add), waits=[tok_mm], sig=1)
            ps_guard[id(ps)] = xo1_toks[o]
        emit_proj("wo1", Ct, wo1_out, extra_mm_waits=[att_last])

        if STAGE == "l1a":
            fins = [e.dma("sync", "d_out", y[i], X[:, i, 1:T + 1],
                          waits=[xo1_toks[i]]) for i in range(NF)]
            e.emit("sync", lambda s: s.nop(), waits=[fins[-1]])
            em.build(sems, block)
            return nc

        tokf1_v, tokf1_t = emit_ffn(1, xo1_toks[NF - 1], att_last, att_last)
        fins = [e.dma("sync", "d_out", y[i], X[:, i, 1:T + 1],
                      waits=[tokf1_v]) for i in range(NF)]
        e.emit("sync", lambda s: s.nop(), waits=[fins[-1]])
        em.build(sems, block)
        return nc

    return nc



def _prep_core_inputs(inputs):
    x = np.asarray(inputs["L0_attn_hidden_states"], np.float32)

    def tileT(w):
        wt = np.ascontiguousarray(np.asarray(w, np.float32).T)
        nin, nout = wt.shape
        return np.ascontiguousarray(wt.reshape(nin, nout // P, P).transpose(1, 0, 2))

    W = {
        "wv0": tileT(inputs["L0_attn_Wv"]), "wk0": tileT(inputs["L0_attn_Wk"]),
        "wr0": tileT(inputs["L0_attn_Wr"]), "wo0": tileT(inputs["L0_attn_Wo"]),
        "wq1": tileT(inputs["L1_attn_Wq"]), "wk1": tileT(inputs["L1_attn_Wk"]),
        "wv1": tileT(inputs["L1_attn_Wv"]), "wo1": tileT(inputs["L1_attn_Wo"]),
    }
    td = np.ascontiguousarray(np.asarray(inputs["L0_attn_time_decay"], np.float32).T)
    tf = np.ascontiguousarray(np.asarray(inputs["L0_attn_time_first"], np.float32).T)

    def tmx(v):
        return np.ascontiguousarray(np.asarray(v, np.float32).reshape(NF, P).T)

    tmk = tmx(inputs["L0_attn_time_mix_k"])
    tmv = tmx(inputs["L0_attn_time_mix_v"])
    tmr = tmx(inputs["L0_attn_time_mix_r"])
    bqkvo = np.ascontiguousarray(np.concatenate(
        [np.asarray(inputs[k], np.float32).reshape(NF, P).T
         for k in ("L1_attn_bq", "L1_attn_bk", "L1_attn_bv", "L1_attn_bo")], axis=1))

    dh = 64
    theta = 10000.0 ** (-np.arange(0, dh, 2, dtype=np.float64) / dh)
    tpos = np.arange(2048, dtype=np.float64)
    temp = np.floor((np.arange(2048) // 576).astype(np.float64) * 0.5)
    ang = (tpos + temp)[:, None] * theta[None, :]
    cosv = np.cos(ang); sinv = np.sin(ang)
    pidx = np.arange(P)
    fidx = pidx % 32

    def rope_tabs(tk):
        C = cosv[tk][:, fidx].T
        S = sinv[tk][:, fidx].T
        return C.astype(np.float32), S.astype(np.float32)

    Pm = np.zeros((P, P), np.float32)
    for m in range(P):
        base = (m // 64) * 64
        d = m % 64
        if d < 32:
            Pm[base + d + 32, m] = -1.0
        else:
            Pm[base + d - 32, m] = 1.0

    tri = np.ascontiguousarray(np.triu(np.ones((P, P), np.float32)))

    H = np.ones((1, 1), np.float32)
    while H.shape[0] < P:
        H = np.block([[H, H], [H, -H]])
    H = np.ascontiguousarray(H.astype(np.float32))

    def ffn_slices(w, rank):
        wt = np.ascontiguousarray(np.asarray(w, np.float32).T)
        nin, nout = wt.shape
        ndb = nout // 256
        per = ndb // 8
        sl = wt.reshape(nin, ndb, 256).transpose(1, 0, 2)[rank * per:(rank + 1) * per]
        return np.ascontiguousarray(sl).reshape(P, 16384)

    cores = []
    for c in range(8):
        b, h = c // 2, c % 2
        lo = h * T
        xt = np.zeros((D, T + 1), np.float32)
        xt[:, 1:] = x[b, lo:lo + T, :].T
        if lo > 0:
            xt[:, 0] = x[b, lo - 1, :]
        tk = np.arange(lo, lo + T)
        cq, sq = rope_tabs(tk)
        ropes = np.stack([cq / 8.0, sq / 8.0, cq, sq], axis=1)
        pv = np.zeros((P, 2), np.float32)
        pv[:, 0] = -SMC if h == 1 else (-1e5 - SMC)
        pv[:, 1] = 1.0 if h == 1 else 0.0
        d = {
            "xT": np.ascontiguousarray(xt),
            "td": td, "tf": tf, "tmk": tmk, "tmv": tmv, "tmr": tmr,
            "bqkvo": bqkvo,
            "ropes": np.ascontiguousarray(ropes),
            "tri": tri, "hadp": H, "hadn": np.ascontiguousarray(-H), "perm": Pm,
            "pvflag": pv,
            "wg0_sl": ffn_slices(inputs["L0_ffn_w_gate"], c),
            "wu0_sl": ffn_slices(inputs["L0_ffn_w_up"], c),
            "wd0_sl": ffn_slices(inputs["L0_ffn_w_down"], c),
            "wg1_sl": ffn_slices(inputs["L1_ffn_w_gate"], c),
            "wu1_sl": ffn_slices(inputs["L1_ffn_w_up"], c),
            "wd1_sl": ffn_slices(inputs["L1_ffn_w_down"], c),
        }
        d.update(W)
        cores.append(d)
    return cores


def kernel(**inputs):
    cores = _prep_core_inputs(inputs)
    nc = build_program()
    res = run_bass_kernel_spmd(nc, cores, core_ids=list(range(8)))
    out = np.zeros((4, 2048, 2048), np.float32)
    for c in range(8):
        b, h = c // 2, c % 2
        yv = np.asarray(res.results[c]["y"], np.float32).reshape(D, T)
        out[b, h * T:(h + 1) * T, :] = yv.T
    return out
